# version 07
# baseline (speedup 1.0000x reference)
import contextlib
import numpy as np
import ml_dtypes
import concourse.bass as bass
import concourse.mybir as mybir
from concourse.bass_utils import run_bass_kernel_spmd

F32 = mybir.dt.float32
BF16 = mybir.dt.bfloat16
AF = mybir.ActivationFunctionType
ALU = mybir.AluOpType
AX = mybir.AxisListType
BF = ml_dtypes.bfloat16

NCORE = 8
S = 8192
TL = 1024
NT = 8
D = 4096
DC = 32
HQ, HKV, DH = 32, 8, 128
KVD = 1024
RANK = 32
E = 8
RSQ = float(1.0 / np.sqrt(np.float64(128.0)))


class Rec:
    def __init__(self):
        self.ops = {e: [] for e in ("t", "v", "s", "g", "y")}
        self.sem = {}
        self.h = {}

    def emit(self, eng, fn):
        self.ops[eng].append(fn)

    def inc(self, name, by=1):
        self.sem[name] = self.sem.get(name, 0) + by
        return self.sem[name]

    def val(self, name):
        return self.sem.get(name, 0)

    def wait(self, eng, name, value=None):
        v = self.val(name) if value is None else value
        if v <= 0:
            return
        h = self.h[name]
        self.emit(eng, lambda e, s=h, v=v: e.wait_ge(s, v))


def build_nc():
    nc = bass.Bass("TRN2", target_bir_lowering=False, detect_race_conditions=False)

    x_in = nc.dram_tensor("x", [TL, D], F32, kind="ExternalInput")
    wq_in = nc.dram_tensor("wq", [D, D], F32, kind="ExternalInput")
    wk_in = nc.dram_tensor("wk", [D, KVD], F32, kind="ExternalInput")
    wv_in = nc.dram_tensor("wv", [D, KVD], F32, kind="ExternalInput")
    wo_in = nc.dram_tensor("wo", [D, D], F32, kind="ExternalInput")
    wf_in = nc.dram_tensor("wf", [D, D], F32, kind="ExternalInput")
    bf_in = nc.dram_tensor("bf", [D], F32, kind="ExternalInput")
    af_in = nc.dram_tensor("af", [D, RANK], F32, kind="ExternalInput")
    bm_in = nc.dram_tensor("bm", [RANK, D], F32, kind="ExternalInput")
    lim_in = nc.dram_tensor("lim", [E, RANK], F32, kind="ExternalInput")
    sc_in = nc.dram_tensor("sincos", [2, 128, TL], F32, kind="ExternalInput")
    out_t = nc.dram_tensor("out", [TL, D], F32, kind="ExternalOutput")

    wt_dram = nc.dram_tensor("wt_dram", [D, D], BF16)
    g_dram = nc.dram_tensor("g_dram", [TL, D], BF16)
    ar_in = nc.dram_tensor("ar_in", [2, 1], F32)
    ar_out = nc.dram_tensor("ar_out", [2, 1], F32)

    ident_bf = nc.inline_tensor(np.eye(128, dtype=BF), name="ident_bf")
    ident_f = nc.inline_tensor(np.eye(128, dtype=np.float32), name="ident_f")
    iota8_c = nc.inline_tensor(np.tile(np.arange(8, dtype=np.float32), (128, 1)), name="iota8")
    ones1f_c = nc.inline_tensor(np.ones((1, 128), np.float32), name="ones1f")

    es = contextlib.ExitStack()
    sb = lambda *a: es.enter_context(nc.sbuf_tensor(*a))
    psr = lambda *a: es.enter_context(nc.psum_tensor(*a))

    arenaA = sb("arenaA", [128, DC * TL], BF16)
    arenaB = sb("arenaB", [128, DC * TL], BF16)
    ktok = sb("ktok", [128, NT * KVD], BF16)
    vtok = sb("vtok", [128, NT * KVD], BF16)
    wslab = sb("wslab", [128, 8192], BF16)
    prodb = sb("prodb", [128, 4096], BF16)

    ttmp = sb("ttmp", [128, 1024], BF16)
    ttmp2 = sb("ttmp2", [128, 1024], BF16)
    gstr = sb("gstr", [128, 2 * NT * 128], BF16)
    xstr = sb("xstr", [128, 2 * NT * 128], F32)
    ostr = sb("ostr", [128, NT * 128], F32)
    sumx = sb("sumx", [128, 1], F32)
    ssq = sb("ssq", [128, 1], F32)
    mven = sb("mven", [128, 4], F32)
    mx8 = sb("mx8", [E, 2], F32)
    mx2 = sb("mx2", [2, 1], F32)
    s12r = sb("s12r", [1, 2], F32)
    s12b = sb("s12b", [128, 2], F32)
    w8t = sb("w8t", [128, DC * E], BF16)
    w8row = prodb[0:E, 0:4096]
    b8c = sb("b8c", [E, 1], F32)
    bcol = sb("bcol", [128, DC], F32)
    bcolt = sb("bcolt", [RANK, 128], F32)
    limbf = sb("limbf", [E, RANK], BF16)
    limtbf = sb("limtbf", [RANK, E], BF16)
    bmsl = sb("bmsl", [RANK, 2 * 128], BF16)
    idbf = sb("idbf", [128, 128], BF16)
    idf = sb("idf", [128, 128], F32)
    io8 = sb("io8", [128, 8], F32)
    on1f = sb("on1f", [1, 128], F32)
    gt_a = sb("gt_a", [128, 8], F32)
    gt_b = sb("gt_b", [128, 8], F32)
    gt_c = sb("gt_c", [128, 8], F32)
    gt_d = sb("gt_d", [128, 8], F32)
    gt_e = sb("gt_e", [128, 8], BF16)
    gs_a = sb("gs_a", [128, 1], F32)
    gs_b = sb("gs_b", [128, 1], F32)
    gs_c = sb("gs_c", [128, 1], F32)
    vsp = sb("vsp", [128, 1], F32)
    scsb = prodb[:].bitcast(F32)
    zt = ostr[0:RANK, 0:TL]
    zanch = prodb[0:RANK, 2048:2560]
    wft = prodb[0:E, 3072:4096]
    gmul = prodb[0:RANK, 1024:2048]
    abf = gstr[:, 0:DC * RANK]
    hsl = xstr[0:E, 0:512]
    dsl = xstr[0:E, 512:1024]
    sc8 = ttmp[:, 0:512].bitcast(F32)
    e8 = ttmp[:, 512:1024].bitcast(F32)
    p8 = ttmp2[:, 0:256]
    sig8 = ttmp2[:, 256:320].bitcast(F32)
    rsg8 = ttmp2[:, 320:384].bitcast(F32)
    rp_a = xstr[:, 0:1024]
    rp_b = xstr[:, 1024:2048]

    pj = [psr(f"pj{i}", [128, 512], F32) for i in range(4)]
    pT = psr("pT", [128, 2048], BF16)
    pg = psr("pg", [128, 512], F32)
    ph = psr("ph", [128, 512], F32)

    r = Rec()
    for nm in ("LW", "LS", "MM", "VE", "SE", "CC", "ST"):
        r.h[nm] = es.enter_context(nc.semaphore(nm))

    def dma(eng, qsem, out, in_):
        v = r.inc(qsem, 16)
        h = r.h[qsem]
        r.emit(eng, lambda e, o=out, i=in_, h=h: e.dma_start(out=o, in_=i).then_inc(h, 16))
        return v

    def emit(eng, fn):
        r.emit(eng, fn)

    def emit_inc(eng, fn, sem_name):
        v = r.inc(sem_name)
        h = r.h[sem_name]
        r.emit(eng, lambda e, f=fn, h=h: f(e).then_inc(h, 1))
        return v

    def vgap(n=2):
        for _ in range(n):
            r.emit("v", lambda e: e.memset(vsp[:], 0.0))

    def emit_inc_dma_safe(fn, sem_name, gaps=3):
        r.emit("v", fn)
        vgap(gaps)
        return emit_inc("v", lambda e: e.memset(vsp[:], 0.0), sem_name)

    xnT3 = arenaA[:].rearrange("p (dc t) -> p dc t", dc=DC)

    dma("g", "LW", idbf[:], ident_bf[:, :])
    dma("g", "LW", idf[:], ident_f[:, :])
    dma("g", "LW", io8[:], iota8_c[:, :])
    dma("g", "LW", on1f[:], ones1f_c[:, :])
    dma("g", "LW", limbf[:], lim_in[:, :])
    dma("g", "LW", abf[:].rearrange("p (dc rr) -> p dc rr", dc=DC),
        af_in.ap().rearrange("(dc p) rr -> p dc rr", p=128))
    dma("g", "LW", w8row[:, :], wf_in[0:E, :])
    dma("g", "LW", b8c[:], bf_in.ap()[0:E].unsqueeze(1))
    v_consts = dma("g", "LW", bcolt[:], bf_in.ap().rearrange("(dc p) -> dc p", p=128))

    sq_scr = arenaB[:, 0:8192].bitcast(F32)
    x_ld = [arenaB[:, 8192 + j * 8192: 8192 + (j + 1) * 8192].bitcast(F32) for j in range(2)]
    xn_tok = arenaB[:, 24576:24576 + 4096]

    v_xload = [None] * NT
    v_tile_users = [0] * NT
    v_tile_se = [0] * NT
    v_xload[0] = dma("y", "LS", x_ld[0][:, :], x_in[0:128, :])
    v_xload[1] = dma("y", "LS", x_ld[1][:, :], x_in[128:256, :])

    for i in range(NT):
        xb = x_ld[i % 2]
        r.wait("s", "LS", v_xload[i])
        if i >= 1:
            r.wait("s", "VE")
        emit("s", lambda e, xb=xb: e.activation(
            sq_scr[:, :], xb[:, :], AF.Square, accum_out=ssq[:]))
        v_se_sum = emit_inc("s", lambda e, xb=xb: e.activation(
            sq_scr[:, :], xb[:, :], AF.Copy, accum_out=sumx[:]), "SE")
        r.wait("v", "SE", v_se_sum)
        emit("v", lambda e: e.tensor_single_scalar(mven[:, 0:1], sumx[:], -1.0 / D, ALU.mult))
        vgap()
        emit("v", lambda e: e.tensor_scalar(
            mven[:, 1:2], mven[:, 0:1], mven[:, 0:1], 1e-5, ALU.mult, ALU.subtract))
        vgap()
        v_var = emit_inc("v", lambda e: e.scalar_tensor_tensor(
            mven[:, 1:2], ssq[:], 1.0 / D, mven[:, 1:2], ALU.mult, ALU.subtract), "VE")
        r.wait("s", "VE", v_var)
        v_std = emit_inc("s", lambda e: e.activation(mven[:, 2:3], mven[:, 1:2], AF.Sqrt,
                                                     scale=1.0), "SE")
        v_tile_se[i] = v_std
        r.wait("v", "SE", v_std)
        emit("v", lambda e: e.reciprocal(mven[:, 3:4], mven[:, 2:3]))
        vgap()
        v_xn = emit_inc("v", lambda e, xb=xb: e.tensor_scalar(
            xn_tok[:, :], xb[:, :], mven[:, 0:1], mven[:, 3:4], ALU.add, ALU.mult), "VE")
        v_tile_users[i] = v_xn
        if i + 2 < NT:
            r.wait("y", "VE", v_xn)
            r.wait("y", "SE", v_se_sum)
            v_xload[i + 2] = dma("y", "LS", x_ld[i % 2][:, :],
                                 x_in[(i + 2) * 128:(i + 3) * 128, :])
        for rnd in range(2):
            r.wait("t", "VE")
            for j in range(16):
                dc = rnd * 16 + j
                fn = lambda e, dc=dc, j=j: e.transpose(
                    pT[:, j * 128:(j + 1) * 128], xn_tok[:, dc * 128:(dc + 1) * 128], idbf[:, :])
                (emit if j < 15 else (lambda eng, f: emit_inc(eng, f, "MM")))("t", fn)
            r.wait("v", "MM")
            base = rnd * 16
            emit_inc("v", lambda e, base=base, i=i: e.tensor_copy(
                xnT3[:, base:base + 16, i * 128:(i + 1) * 128],
                pT[:].rearrange("p (j f) -> p j f", j=16)), "VE")

    v_xnT_done = r.val("VE")

    wstage = [wslab[:, 0:4096], wslab[:, 4096:8192]]
    wstrip = [arenaB[:, 28672:32768], arenaB[:, 28672:32768]]
    v_wblk = [0] * DC
    v_wuse = [0] * DC
    v_wstore = [0] * DC
    v_wevac = [0] * DC
    v_wblk[0] = dma("g", "LW", wstage[0][:, :], wf_in[0:128, :])
    v_wblk[1] = dma("g", "LW", wstage[1][:, :], wf_in[128:256, :])
    for ob in range(DC):
        src = wstage[ob % 2]
        strip = wstrip[ob % 2]
        for rnd in range(2):
            r.wait("t", "LW", v_wblk[ob])
            r.wait("t", "VE")
            for j in range(16):
                dc = rnd * 16 + j
                fn = lambda e, src=src, dc=dc, j=j: e.transpose(
                    pT[:, j * 128:(j + 1) * 128], src[:, dc * 128:(dc + 1) * 128], idbf[:, :])
                (emit if j < 15 else (lambda eng, f: emit_inc(eng, f, "MM")))("t", fn)
            r.wait("v", "MM")
            if ob >= 1 and rnd == 0:
                r.wait("v", "LS", v_wstore[ob - 1])
            base = rnd * 16
            v_wevac[ob] = emit_inc_dma_safe(lambda e, strip=strip, base=base: e.tensor_copy(
                strip[:].rearrange("p (dc f) -> p dc f", dc=DC)[:, base:base + 16, :],
                pT[:].rearrange("p (j f) -> p j f", j=16)), "VE")
        v_wuse[ob] = r.val("MM")
        r.wait("y", "VE", v_wevac[ob])
        v_wstore[ob] = dma("y", "LS", wt_dram.ap().rearrange(
            "(dc p) o -> p dc o", p=128)[:, :, ob * 128:(ob + 1) * 128],
            strip[:].rearrange("p (dc f) -> p dc f", dc=DC))
        if ob + 2 < DC:
            r.wait("g", "MM", v_wuse[ob])
            v_wblk[ob + 2] = dma("g", "LW", wstage[ob % 2][:, :],
                                 wf_in[(ob + 2) * 128:(ob + 3) * 128, :])
    v_wstage_all_mm = r.val("MM")
    v_wstage_all_st = r.val("LS")

    r.wait("t", "LW", v_consts)
    r.wait("t", "VE")
    for dc in range(DC):
        fn = lambda e, dc=dc: e.transpose(pT[:, dc * 8:(dc + 1) * 8],
                                          w8row[0:E, dc * 128:(dc + 1) * 128], idbf[0:E, 0:E])
        (emit if dc < DC - 1 else (lambda eng, f: emit_inc(eng, f, "MM")))("t", fn)
    emit_inc("t", lambda e: e.transpose(pT[0:RANK, 1024:1024 + 8], limbf[0:E, :], idbf[0:E, 0:E]), "MM")
    r.wait("v", "MM")
    emit("v", lambda e: e.tensor_copy(w8t[:], pT[:, 0:DC * E]))
    emit_inc("v", lambda e: e.tensor_copy(limtbf[:], pT[0:RANK, 1024:1024 + 8]), "VE")
    r.wait("t", "VE")
    emit_inc("t", lambda e: e.transpose(pg[:, 0:DC], bcolt[:], idf[0:RANK, 0:RANK]), "MM")
    r.wait("v", "MM")
    emit_inc("v", lambda e: e.tensor_copy(bcol[:], pg[:, 0:DC]), "VE")

    r.wait("t", "VE")
    for tq in range(2):
        if tq:
            r.wait("t", "VE")
        for dc in range(DC):
            fn = lambda e, tq=tq, dc=dc: e.matmul(
                pg[0:RANK, :], abf[:, dc * RANK:(dc + 1) * RANK],
                arenaA[:, dc * TL + tq * 512: dc * TL + tq * 512 + 512],
                start=(dc == 0), stop=(dc == DC - 1))
            (emit if dc < DC - 1 else (lambda eng, f: emit_inc(eng, f, "MM")))("t", fn)
        r.wait("v", "MM")
        emit_inc("v", lambda e, tq=tq: e.tensor_copy(zt[:, tq * 512:(tq + 1) * 512], pg[0:RANK, :]), "VE")
    for tq in range(2):
        if tq:
            r.wait("t", "VE")
        for dc in range(DC):
            fn = lambda e, tq=tq, dc=dc: e.matmul(
                ph[0:E, :], w8t[:, dc * E:(dc + 1) * E],
                arenaA[:, dc * TL + tq * 512: dc * TL + tq * 512 + 512],
                start=(dc == 0), stop=(dc == DC - 1))
            (emit if dc < DC - 1 else (lambda eng, f: emit_inc(eng, f, "MM")))("t", fn)
        r.wait("v", "MM")
        emit_inc("v", lambda e, tq=tq: e.tensor_scalar_add(
            hsl[:, tq * 256:(tq + 1) * 256],
            ph[0:E, :].rearrange("p (a two) -> p a two", two=2)[:, :, 1], b8c[:]), "VE")
    v_zanch = emit_inc("v", lambda e: e.tensor_copy(
        zanch[:], zt[:].rearrange("p (a two) -> p a two", two=2)[:, :, 1]), "VE")
    r.wait("t", "VE", v_zanch)
    emit_inc("t", lambda e: e.matmul(pg[0:E, :], limtbf[:], zanch[:], start=True, stop=True), "MM")
    r.wait("v", "MM")
    emit("v", lambda e: e.tensor_copy(dsl[:], pg[0:E, :]))
    emit("v", lambda e: e.reduce_max(mx8[:, 0:1], hsl[:], axis=AX.X, apply_absolute_value=True))
    v_mx = emit_inc("v", lambda e: e.reduce_max(mx8[:, 1:2], dsl[:], axis=AX.X,
                                                apply_absolute_value=True), "VE")
    r.wait("t", "VE", v_mx)
    emit_inc("t", lambda e: e.transpose(ph[0:2, 0:E], mx8[0:E, :], idf[0:E, 0:E]), "MM")
    r.wait("v", "MM")
    v_mx2 = emit_inc_dma_safe(lambda e: e.reduce_max(mx2[:], ph[0:2, 0:E], axis=AX.X), "VE")
    r.wait("g", "VE", v_mx2)
    v_arin = dma("g", "LW", ar_in[:, :], mx2[:])
    r.wait("g", "LW", v_arin)
    v_cc = r.inc("CC", 1)
    emit("g", lambda e: e.collective_compute(
        "AllReduce", ALU.max, replica_groups=[list(range(NCORE))],
        ins=[ar_in.ap().opt()], outs=[ar_out.ap().opt()]).then_inc(r.h["CC"], 1))
    r.wait("g", "CC", v_cc)
    v_armax = dma("g", "LW", mx2[:], ar_out[:, :])
    r.wait("v", "LW", v_armax)
    emit("v", lambda e: e.tensor_single_scalar(mx2[:], mx2[:], 1e-6, ALU.max))
    vgap()
    emit("v", lambda e: e.reciprocal(mx2[:], mx2[:]))
    vgap()
    v_sc = emit_inc("v", lambda e: e.tensor_single_scalar(mx2[:], mx2[:], 0.5, ALU.mult), "VE")
    r.wait("t", "VE", v_sc)
    emit_inc("t", lambda e: e.transpose(pg[0:1, 0:2], mx2[0:2, :], idf[0:2, 0:2]), "MM")
    r.wait("v", "MM")
    v_s12 = emit_inc("v", lambda e: e.tensor_copy(s12r[:], pg[0:1, 0:2]), "VE")
    r.wait("t", "VE", v_s12)
    emit_inc("t", lambda e: e.matmul(ph[:, 0:2], on1f[:], s12r[:], start=True, stop=True), "MM")
    r.wait("v", "MM")
    emit_inc("v", lambda e: e.tensor_copy(s12b[:], ph[:, 0:2]), "VE")
    vgap()

    for at in range(4):
        r.wait("t", "VE")
        emit("t", lambda e, at=at: e.transpose(pg[:, 0:E], hsl[0:E, at * 128:(at + 1) * 128],
                                               idf[0:E, 0:E]))
        emit_inc("t", lambda e, at=at: e.transpose(ph[:, 0:E], dsl[0:E, at * 128:(at + 1) * 128],
                                                   idf[0:E, 0:E]), "MM")
        r.wait("v", "MM")
        emit("v", lambda e: e.tensor_scalar_mul(gt_a[:], pg[:, 0:E], s12b[:, 0:1]))
        emit("v", lambda e: e.tensor_scalar_mul(gt_b[:], ph[:, 0:E], s12b[:, 1:2]))
        vgap()
        emit("v", lambda e: e.tensor_tensor(gt_a[:], gt_a[:], gt_b[:], ALU.add))
        vgap()
        v1 = emit_inc("v", lambda e: e.reduce_max(gs_a[:], gt_a[:], axis=AX.X, negate=True), "VE")
        r.wait("s", "VE", v1)
        v2 = emit_inc("s", lambda e: e.activation(gt_b[:], gt_a[:], AF.Exp, bias=gs_a[:],
                                                  scale=1.0, accum_out=gs_b[:]), "SE")
        r.wait("v", "SE", v2)
        emit("v", lambda e: e.reciprocal(gs_b[:], gs_b[:]))
        vgap()
        emit("v", lambda e: e.tensor_scalar_mul(gt_b[:], gt_b[:], gs_b[:]))
        vgap()
        emit("v", lambda e: e.reduce_max(gs_a[:], gt_b[:], axis=AX.X))
        vgap()
        emit("v", lambda e: e.tensor_scalar_sub(gt_c[:], gt_b[:], gs_a[:]))
        vgap()
        emit("v", lambda e: e.tensor_single_scalar(gt_c[:], gt_c[:], 0.0, ALU.is_equal))
        vgap()
        emit("v", lambda e: e.tensor_single_scalar(gt_c[:], gt_c[:], 1.0, ALU.subtract))
        vgap()
        emit("v", lambda e: e.scalar_tensor_tensor(gt_d[:], gt_c[:], -1e30, io8[:],
                                                   ALU.mult, ALU.add))
        vgap()
        emit("v", lambda e: e.tensor_reduce(gs_a[:], gt_d[:], axis=AX.X, op=ALU.min))
        vgap()
        emit("v", lambda e: e.tensor_scalar_sub(gt_c[:], io8[:], gs_a[:]))
        vgap()
        emit("v", lambda e: e.tensor_single_scalar(gt_c[:], gt_c[:], 0.0, ALU.is_equal))
        vgap()
        emit("v", lambda e: e.scalar_tensor_tensor(gt_d[:], gt_c[:], -1e30, gt_b[:],
                                                   ALU.mult, ALU.add))
        vgap()
        emit("v", lambda e: e.reduce_max(gs_a[:], gt_d[:], axis=AX.X))
        vgap()
        v3 = emit_inc("v", lambda e: e.tensor_single_scalar(gs_a[:], gs_a[:], -2.0, ALU.mult), "VE")
        r.wait("s", "VE", v3)
        v4 = emit_inc("s", lambda e: e.activation(gt_c[:], gt_b[:], AF.Sigmoid, bias=gs_a[:],
                                                  scale=2.0), "SE")
        r.wait("v", "SE", v4)
        emit("v", lambda e: e.tensor_tensor(gt_d[:], gt_b[:], gt_c[:], ALU.mult))
        vgap()
        emit("v", lambda e: e.reduce_sum(gs_c[:], gt_d[:], axis=AX.X))
        vgap()
        emit("v", lambda e: e.tensor_single_scalar(gs_c[:], gs_c[:], 1e-9, ALU.add))
        vgap()
        emit("v", lambda e: e.reciprocal(gs_c[:], gs_c[:]))
        vgap()
        v7 = emit_inc("v", lambda e: e.tensor_scalar_mul(gt_e[:], gt_d[:], gs_c[:]), "VE")
        r.wait("t", "VE", v7)
        emit_inc("t", lambda e: e.transpose(pT[0:E, 0:128], gt_e[:], idbf[:, :]), "MM")
        r.wait("v", "MM")
        emit_inc("v", lambda e, at=at: e.tensor_copy(
            wft[:, at * 256:(at + 1) * 256].rearrange("p (a b) -> p a b", b=2),
            pT[0:E, 0:128].unsqueeze(2).broadcast_to([E, 128, 2])), "VE")

    for tq in range(2):
        r.wait("t", "VE")
        emit_inc("t", lambda e, tq=tq: e.matmul(pg[0:RANK, :], limbf[0:E, :],
                                                wft[:, tq * 512:(tq + 1) * 512],
                                                start=True, stop=True), "MM")
        r.wait("v", "MM")
        emit_inc("v", lambda e, tq=tq: e.scalar_tensor_tensor(
            gmul[:, tq * 512:(tq + 1) * 512], zt[:, tq * 512:(tq + 1) * 512], 0.5,
            pg[0:RANK, :], ALU.mult, ALU.mult), "VE")
    v_gmul = r.val("VE")

    hslab = [wslab[:, 0:4096], wslab[:, 4096:8192]]
    gt_tmp = [ttmp, ttmp2]
    v_hblk = [0] * DC
    v_hcomp = [0] * DC
    v_hsilu = [0] * DC
    v_gevac = [0] * DC
    v_gstore = [0] * DC
    r.wait("y", "MM", v_wstage_all_mm)
    v_bm = [0] * DC
    v_bm[0] = dma("g", "LW", bmsl[:, 0:128], bm_in[:, 0:128])
    v_bm[1] = dma("g", "LW", bmsl[:, 128:256], bm_in[:, 128:256])
    v_hblk[0] = dma("y", "LS", hslab[0][:].rearrange("p (dc o) -> p dc o", dc=DC),
                    wt_dram.ap().rearrange("(dc p) o -> p dc o", p=128)[:, :, 0:128])
    v_hblk[1] = dma("y", "LS", hslab[1][:].rearrange("p (dc o) -> p dc o", dc=DC),
                    wt_dram.ap().rearrange("(dc p) o -> p dc o", p=128)[:, :, 128:256])
    for ob in range(DC):
        r.wait("t", "LS", v_hblk[ob])
        r.wait("t", "LW", v_bm[ob])
        if ob == 0:
            r.wait("t", "VE", v_gmul)
        banks = [pj[(ob % 2) * 2], pj[(ob % 2) * 2 + 1]]
        for tq in range(2):
            bank = banks[tq]
            for dc in range(DC):
                fn = lambda e, ob=ob, tq=tq, dc=dc, bank=bank: e.matmul(
                    bank[:], hslab[ob % 2][:, dc * 128:(dc + 1) * 128],
                    arenaA[:, dc * TL + tq * 512: dc * TL + tq * 512 + 512],
                    start=(dc == 0), stop=False)
                emit("t", fn)
            fn = lambda e, ob=ob, tq=tq, bank=bank: e.matmul(
                bank[:], bmsl[:, (ob % 2) * 128:(ob % 2) * 128 + 128],
                gmul[:, tq * 512:(tq + 1) * 512], start=False, stop=True)
            emit_inc("t", fn, "MM")
            r.wait("s", "MM")
            v_hsilu[ob] = emit_inc("s", lambda e, ob=ob, tq=tq, bank=bank: e.activation(
                gt_tmp[ob % 2][:, tq * 512:(tq + 1) * 512], bank[:], AF.Silu,
                bias=bcol[:, ob:ob + 1]), "SE")
        v_hcomp[ob] = r.val("MM")
        r.wait("t", "SE", v_hsilu[ob])
        r.wait("t", "VE")
        for j in range(NT):
            fn = lambda e, ob=ob, j=j: e.transpose(
                pT[:, j * 128:(j + 1) * 128], gt_tmp[ob % 2][:, j * 128:(j + 1) * 128], idbf[:, :])
            (emit if j < NT - 1 else (lambda eng, f: emit_inc(eng, f, "MM")))("t", fn)
        r.wait("v", "MM")
        gsl = gstr[:, (ob % 2) * 1024:(ob % 2) * 1024 + 1024]
        if ob >= 2:
            r.wait("v", "LS", v_gstore[ob - 2])
        v_gevac[ob] = emit_inc_dma_safe(lambda e, gsl=gsl: e.tensor_copy(gsl[:, :], pT[:, 0:NT * 128]), "VE")
        r.wait("y", "VE", v_gevac[ob])
        v_gstore[ob] = dma("y", "LS",
                           g_dram.ap().rearrange("(i p) o -> p i o", p=128)[:, :, ob * 128:(ob + 1) * 128],
                           gsl[:].rearrange("p (i f) -> p i f", i=NT))
        if ob + 2 < DC:
            r.wait("y", "MM", v_hcomp[ob])
            v_hblk[ob + 2] = dma("y", "LS", hslab[ob % 2][:].rearrange("p (dc o) -> p dc o", dc=DC),
                                 wt_dram.ap().rearrange("(dc p) o -> p dc o", p=128)[:, :, (ob + 2) * 128:(ob + 3) * 128])
            r.wait("g", "MM", v_hcomp[ob])
            v_bm[ob + 2] = dma("g", "LW", bmsl[:, (ob % 2) * 128:(ob % 2) * 128 + 128],
                               bm_in[:, (ob + 2) * 128:(ob + 3) * 128])
    v_h_done_mm = r.val("MM")
    r.wait("y", "MM", v_h_done_mm)
    v_sincos = dma("y", "LS", scsb[:].rearrange("p (s f) -> p s f", s=2),
                   sc_in.ap().rearrange("s p f -> p s f"))

    qslab = [wslab[:, 0:4096], wslab[:, 4096:8192]]

    def emit_projection(w_dram, nblocks, dst_tok, dstw, rope, first_gate_mm):
        v_ld = [0] * nblocks
        v_cmp = [0] * nblocks
        r.wait("g", "MM", first_gate_mm)
        v_ld[0] = dma("g", "LW", qslab[0][:].rearrange("p (dc o) -> p dc o", dc=DC),
                      w_dram.ap().rearrange("(dc p) o -> p dc o", p=128)[:, :, 0:128])
        if nblocks > 1:
            v_ld[1] = dma("g", "LW", qslab[1][:].rearrange("p (dc o) -> p dc o", dc=DC),
                          w_dram.ap().rearrange("(dc p) o -> p dc o", p=128)[:, :, 128:256])
        for ob in range(nblocks):
            r.wait("t", "LW", v_ld[ob])
            banks = [pj[(ob % 2) * 2], pj[(ob % 2) * 2 + 1]]
            for tq in range(2):
                bank = banks[tq]
                for dc in range(DC):
                    fn = lambda e, ob=ob, tq=tq, dc=dc, bank=bank: e.matmul(
                        bank[:], qslab[ob % 2][:, dc * 128:(dc + 1) * 128],
                        arenaA[:, dc * TL + tq * 512: dc * TL + tq * 512 + 512],
                        start=(dc == 0), stop=(dc == DC - 1))
                    (emit if dc < DC - 1 else (lambda eng, f: emit_inc(eng, f, "MM")))("t", fn)
            v_cmp[ob] = r.val("MM")
            r.wait("v", "MM")
            emit("v", lambda e, banks=banks: e.tensor_copy(ttmp[:, 0:512], banks[0][:]))
            v_ev = emit_inc("v", lambda e, banks=banks: e.tensor_copy(ttmp[:, 512:1024], banks[1][:]), "VE")
            r.wait("t", "VE", v_ev)
            for j in range(NT):
                fn = lambda e, j=j: e.transpose(pT[:, j * 128:(j + 1) * 128],
                                                ttmp[:, j * 128:(j + 1) * 128], idbf[:, :])
                (emit if j < NT - 1 else (lambda eng, f: emit_inc(eng, f, "MM")))("t", fn)
            r.wait("v", "MM")
            dst3 = dst_tok[:].rearrange("p (i w) -> p i w", i=NT)
            if not rope:
                emit_inc("v", lambda e, dst3=dst3, ob=ob: e.tensor_copy(
                    dst3[:, :, ob * 128:(ob + 1) * 128],
                    pT[:, 0:NT * 128].rearrange("p (i f) -> p i f", i=NT)), "VE")
            else:
                pt3 = pT[:, 0:NT * 128].rearrange("p (i f) -> p i f", i=NT)
                rpa3 = rp_a[:].rearrange("p (i f) -> p i f", i=NT)
                sin3 = scsb[:, TL:2 * TL].rearrange("p (i f) -> p i f", i=NT)
                emit("v", lambda e, pt3=pt3, rpa3=rpa3, sin3=sin3: e.tensor_tensor(
                    rpa3[:, :, 0:64], pt3[:, :, 64:128], sin3[:, :, 0:64], ALU.mult))
                emit("v", lambda e, pt3=pt3, rpa3=rpa3, sin3=sin3: e.tensor_tensor(
                    rpa3[:, :, 64:128], pt3[:, :, 0:64], sin3[:, :, 64:128], ALU.mult))
                emit("v", lambda e: e.tensor_tensor(rp_b[:], pT[:, 0:1024], scsb[:, 0:TL], ALU.mult))
                emit_inc("v", lambda e, dst3=dst3, ob=ob: e.tensor_tensor(
                    dst3[:, :, ob * 128:(ob + 1) * 128],
                    rp_a[:].rearrange("p (i f) -> p i f", i=NT),
                    rp_b[:].rearrange("p (i f) -> p i f", i=NT), ALU.add), "VE")
            if ob + 2 < nblocks:
                r.wait("g", "MM", v_cmp[ob])
                v_ld[ob + 2] = dma("g", "LW", qslab[ob % 2][:].rearrange("p (dc o) -> p dc o", dc=DC),
                                   w_dram.ap().rearrange("(dc p) o -> p dc o", p=128)[:, :, (ob + 2) * 128:(ob + 3) * 128])
        return r.val("MM")

    r.wait("v", "LS", v_sincos)
    v_after_k = emit_projection(wk_in, HKV, ktok, KVD, True, v_h_done_mm)
    v_after_v = emit_projection(wv_in, HKV, vtok, KVD, False, v_after_k)
    v_after_q = emit_projection(wq_in, DC, arenaB, D, True, v_after_v)
    v_qkv_done = r.val("VE")

    pf = xstr[:]
    p8f = ostr[:, 0:256]
    aof = [ostr[:, 256:512], ostr[:, 512:768]]
    v_exp_prev = 0
    for i in range(NT):
        qv = arenaB[:, i * D:(i + 1) * D]
        kv = ktok[:, i * KVD:(i + 1) * KVD]
        vv = vtok[:, i * KVD:(i + 1) * KVD]
        if i == 0:
            r.wait("v", "VE", v_qkv_done)
        if v_exp_prev:
            r.wait("v", "SE", v_exp_prev)
        for hp in range(16):
            emit("v", lambda e, qv=qv, kv=kv, hp=hp: e.tensor_tensor(
                pf.rearrange("p (h g d) -> p h g d", h=2, g=E),
                qv[:, hp * 256:(hp + 1) * 256].rearrange("p (h d) -> p h d", h=2)
                  .unsqueeze(2).broadcast_to([128, 2, E, DH]),
                kv[:].rearrange("p (g d) -> p g d", g=E)
                  .unsqueeze(1).broadcast_to([128, 2, E, DH]), ALU.mult))
            fn = lambda e, hp=hp: e.tensor_reduce(
                sc8[:, hp * 16:(hp + 1) * 16].rearrange("p (h g) -> p h g", h=2).unsqueeze(3),
                pf.rearrange("p (h g d) -> p h g d", h=2, g=E),
                axis=AX.X, op=ALU.add)
            if hp < 15:
                emit("v", fn)
            else:
                emit_inc("v", fn, "VE")
        r.wait("s", "VE")
        v_exp_prev = emit_inc("s", lambda e: e.activation(e8[:], sc8[:], AF.Exp, scale=RSQ), "SE")
        r.wait("v", "SE", v_exp_prev)
        emit("v", lambda e: e.tensor_reduce(
            sig8[:].unsqueeze(2), e8[:].rearrange("p (h g) -> p h g", h=HQ),
            axis=AX.X, op=ALU.add))
        vgap()
        emit("v", lambda e: e.reciprocal(rsg8[:], sig8[:]))
        vgap()
        emit("v", lambda e: e.tensor_tensor(
            p8f.rearrange("p (h g) -> p h g", h=HQ),
            e8[:].rearrange("p (h g) -> p h g", h=HQ),
            rsg8[:].unsqueeze(2).broadcast_to([128, HQ, E]), ALU.mult))
        vgap()
        for hp in range(16):
            emit("v", lambda e, vv=vv, hp=hp: e.tensor_tensor(
                pf.rearrange("p (h d g) -> p h d g", h=2, d=DH),
                p8f[:, hp * 16:(hp + 1) * 16].rearrange("p (h g) -> p h g", h=2)
                  .unsqueeze(2).broadcast_to([128, 2, DH, E]),
                vv[:].rearrange("p (g d) -> p g d", g=E).transpose([0, 2, 1])
                  .unsqueeze(1).broadcast_to([128, 2, DH, E]), ALU.mult))
            emit("v", lambda e, hp=hp: e.tensor_reduce(
                aof[hp % 2].rearrange("p (h d) -> p h d", h=2).unsqueeze(3),
                pf.rearrange("p (h d g) -> p h d g", h=2, d=DH),
                axis=AX.X, op=ALU.add))
            if hp >= 1:
                emit("v", lambda e, qv=qv, hp=hp: e.tensor_copy(
                    qv[:, (hp - 1) * 256:hp * 256], aof[(hp - 1) % 2]))
        vgap(2)
        emit_inc("v", lambda e, qv=qv: e.tensor_copy(qv[:, 15 * 256:16 * 256], aof[15 % 2]), "VE")
        v_ao = r.val("VE")
        for rnd in range(2):
            r.wait("t", "VE")
            for j in range(16):
                dc = rnd * 16 + j
                fn = lambda e, qv=qv, dc=dc, j=j: e.transpose(
                    pT[:, j * 128:(j + 1) * 128], qv[:, dc * 128:(dc + 1) * 128], idbf[:, :])
                (emit if j < 15 else (lambda eng, f: emit_inc(eng, f, "MM")))("t", fn)
            r.wait("v", "MM")
            base = rnd * 16
            emit_inc("v", lambda e, base=base, i=i: e.tensor_copy(
                xnT3[:, base:base + 16, i * 128:(i + 1) * 128],
                pT[:].rearrange("p (j f) -> p j f", j=16)), "VE")
    v_att_done = r.val("VE")
    v_att_done_mm = r.val("MM")

    v_ld = [0] * DC
    v_cmp = [0] * DC
    v_gld = [0] * DC
    v_xld2 = [0] * DC
    v_comb = [0] * DC
    v_ost = [0] * DC
    r.wait("g", "MM", v_after_q)
    v_ld[0] = dma("g", "LW", qslab[0][:].rearrange("p (dc o) -> p dc o", dc=DC),
                  wo_in.ap().rearrange("(dc p) o -> p dc o", p=128)[:, :, 0:128])
    v_ld[1] = dma("g", "LW", qslab[1][:].rearrange("p (dc o) -> p dc o", dc=DC),
                  wo_in.ap().rearrange("(dc p) o -> p dc o", p=128)[:, :, 128:256])
    r.wait("y", "VE", v_att_done)
    v_gld[0] = dma("y", "LS", gstr[:, 0:1024].rearrange("p (i f) -> p i f", i=NT),
                   g_dram.ap().rearrange("(i p) o -> p i o", p=128)[:, :, 0:128])
    v_xld2[0] = dma("y", "LS", xstr[:, 0:1024].rearrange("p (i f) -> p i f", i=NT),
                    x_in.ap().rearrange("(i p) o -> p i o", p=128)[:, :, 0:128])
    v_gld[1] = dma("y", "LS", gstr[:, 1024:2048].rearrange("p (i f) -> p i f", i=NT),
                   g_dram.ap().rearrange("(i p) o -> p i o", p=128)[:, :, 128:256])
    v_xld2[1] = dma("y", "LS", xstr[:, 1024:2048].rearrange("p (i f) -> p i f", i=NT),
                    x_in.ap().rearrange("(i p) o -> p i o", p=128)[:, :, 128:256])
    for ob in range(DC):
        r.wait("t", "LW", v_ld[ob])
        if ob == 0:
            r.wait("t", "VE", v_att_done)
        banks = [pj[(ob % 2) * 2], pj[(ob % 2) * 2 + 1]]
        for tq in range(2):
            bank = banks[tq]
            for dc in range(DC):
                fn = lambda e, ob=ob, tq=tq, dc=dc, bank=bank: e.matmul(
                    bank[:], qslab[ob % 2][:, dc * 128:(dc + 1) * 128],
                    arenaA[:, dc * TL + tq * 512: dc * TL + tq * 512 + 512],
                    start=(dc == 0), stop=(dc == DC - 1))
                (emit if dc < DC - 1 else (lambda eng, f: emit_inc(eng, f, "MM")))("t", fn)
        v_cmp[ob] = r.val("MM")
        r.wait("v", "MM")
        emit("v", lambda e, banks=banks: e.tensor_copy(ttmp[:, 0:512], banks[0][:]))
        v_ev = emit_inc("v", lambda e, banks=banks: e.tensor_copy(ttmp[:, 512:1024], banks[1][:]), "VE")
        r.wait("t", "VE", v_ev)
        for j in range(NT):
            fn = lambda e, j=j: e.transpose(pT[:, j * 128:(j + 1) * 128],
                                            ttmp[:, j * 128:(j + 1) * 128], idbf[:, :])
            (emit if j < NT - 1 else (lambda eng, f: emit_inc(eng, f, "MM")))("t", fn)
        r.wait("v", "MM")
        r.wait("v", "LS", max(v_gld[ob], v_xld2[ob]))
        half = ob % 2
        if ob >= 1:
            r.wait("v", "ST", v_ost[ob - 1])
        emit("v", lambda e, half=half: e.tensor_tensor(
            ostr[:], pT[:, 0:NT * 128], gstr[:, half * 1024:half * 1024 + 1024], ALU.mult))
        v_comb[ob] = emit_inc_dma_safe(lambda e, half=half: e.tensor_tensor(
            ostr[:], ostr[:],
            xstr[:, half * 1024:half * 1024 + 1024], ALU.add), "VE")
        r.wait("y", "VE", v_comb[ob])
        v_ost[ob] = r.inc("ST", 16)
        emit("y", lambda e, ob=ob, half=half: e.dma_start(
            out=out_t.ap().rearrange("(i p) o -> p i o", p=128)[:, :, ob * 128:(ob + 1) * 128],
            in_=ostr[:].rearrange("p (i f) -> p i f", i=NT))
            .then_inc(r.h["ST"], 16))
        if ob + 2 < DC:
            r.wait("g", "MM", v_cmp[ob])
            v_ld[ob + 2] = dma("g", "LW", qslab[ob % 2][:].rearrange("p (dc o) -> p dc o", dc=DC),
                               wo_in.ap().rearrange("(dc p) o -> p dc o", p=128)[:, :, (ob + 2) * 128:(ob + 3) * 128])
            r.wait("y", "VE", v_comb[ob])
            v_gld[ob + 2] = dma("y", "LS", gstr[:, half * 1024:half * 1024 + 1024].rearrange("p (i f) -> p i f", i=NT),
                                g_dram.ap().rearrange("(i p) o -> p i o", p=128)[:, :, (ob + 2) * 128:(ob + 3) * 128])
            v_xld2[ob + 2] = dma("y", "LS", xstr[:, half * 1024:half * 1024 + 1024].rearrange("p (i f) -> p i f", i=NT),
                                 x_in.ap().rearrange("(i p) o -> p i o", p=128)[:, :, (ob + 2) * 128:(ob + 3) * 128])

    r.wait("y", "ST")

    with es, nc.Block() as block:
        @block.tensor
        def _(t):
            for fn in r.ops["t"]:
                fn(t)

        @block.vector
        def _(v):
            for fn in r.ops["v"]:
                fn(v)

        @block.scalar
        def _(s):
            for fn in r.ops["s"]:
                fn(s)

        @block.gpsimd
        def _(g):
            for fn in r.ops["g"]:
                fn(g)

        @block.sync
        def _(y):
            for fn in r.ops["y"]:
                fn(y)

    return nc


def _rope_tables():
    import jax
    import jax.numpy as jnp
    cpu = jax.devices("cpu")[0]
    with jax.default_device(cpu):
        i = jnp.arange(64, dtype=jnp.float32)
        emb = 10000.0 ** (2.0 * jnp.floor(i / 2.0) / 64)
        pos = jnp.arange(S, dtype=jnp.float32)
        freqs = pos[:, None] * emb[None, :]
        sin = jnp.concatenate([jnp.sin(freqs), jnp.sin(freqs)], axis=-1)
        cos = jnp.concatenate([jnp.cos(freqs), jnp.cos(freqs)], axis=-1)
        sin = np.asarray(sin, np.float32)
        cos = np.asarray(cos, np.float32)
    sinmod = sin.copy()
    sinmod[:, :64] = -sinmod[:, :64]
    return cos, sinmod


def kernel(L0_attn_hidden_states, L0_attn_Wq, L0_attn_Wk, L0_attn_Wv, L0_attn_Wo,
           L0_ffn_W, L0_ffn_b, L0_ffn_A, L0_ffn_Bmat, L0_ffn_LiMEs):
    x = np.ascontiguousarray(np.asarray(L0_attn_hidden_states, np.float32)[0])
    cos, sinmod = _rope_tables()
    nc = build_nc()
    in_maps = []
    for c in range(NCORE):
        sl = slice(c * TL, (c + 1) * TL)
        ct = cos[sl].reshape(NT, 128, 128).transpose(1, 0, 2).reshape(128, TL)
        st_ = sinmod[sl].reshape(NT, 128, 128).transpose(1, 0, 2).reshape(128, TL)
        in_maps.append({
            "x": np.ascontiguousarray(x[sl]),
            "wq": np.asarray(L0_attn_Wq, np.float32),
            "wk": np.asarray(L0_attn_Wk, np.float32),
            "wv": np.asarray(L0_attn_Wv, np.float32),
            "wo": np.asarray(L0_attn_Wo, np.float32),
            "wf": np.asarray(L0_ffn_W, np.float32),
            "bf": np.asarray(L0_ffn_b, np.float32),
            "af": np.asarray(L0_ffn_A, np.float32),
            "bm": np.asarray(L0_ffn_Bmat, np.float32),
            "lim": np.asarray(L0_ffn_LiMEs, np.float32),
            "sincos": np.ascontiguousarray(np.stack([ct, st_])),
        })
    res = run_bass_kernel_spmd(nc, in_maps, core_ids=list(range(NCORE)))
    out = np.concatenate([res.results[c]["out"] for c in range(NCORE)], axis=0)
    return out[None]


# version 09
# speedup vs baseline: 1.0504x; 1.0504x over previous
import contextlib
import numpy as np
import ml_dtypes
import concourse.bass as bass
import concourse.mybir as mybir
from concourse.bass_utils import run_bass_kernel_spmd

F32 = mybir.dt.float32
BF16 = mybir.dt.bfloat16
AF = mybir.ActivationFunctionType
ALU = mybir.AluOpType
AX = mybir.AxisListType
BF = ml_dtypes.bfloat16

NCORE = 8
S = 8192
TL = 1024
NT = 8
D = 4096
DC = 32
HQ, HKV, DH = 32, 8, 128
KVD = 1024
RANK = 32
E = 8
RSQ = float(1.0 / np.sqrt(np.float64(128.0)))


class Rec:
    def __init__(self):
        self.ops = {e: [] for e in ("t", "v", "s", "g", "y")}
        self.sem = {}
        self.h = {}

    def emit(self, eng, fn):
        self.ops[eng].append(fn)

    def inc(self, name, by=1):
        self.sem[name] = self.sem.get(name, 0) + by
        return self.sem[name]

    def val(self, name):
        return self.sem.get(name, 0)

    def wait(self, eng, name, value=None):
        v = self.val(name) if value is None else value
        if v <= 0:
            return
        h = self.h[name]
        self.emit(eng, lambda e, s=h, v=v: e.wait_ge(s, v))


def build_nc():
    nc = bass.Bass("TRN2", target_bir_lowering=False, detect_race_conditions=False)

    x_in = nc.dram_tensor("x", [TL, D], F32, kind="ExternalInput")
    wq_in = nc.dram_tensor("wq", [D, D], F32, kind="ExternalInput")
    wk_in = nc.dram_tensor("wk", [D, KVD], F32, kind="ExternalInput")
    wv_in = nc.dram_tensor("wv", [D, KVD], F32, kind="ExternalInput")
    wo_in = nc.dram_tensor("wo", [D, D], F32, kind="ExternalInput")
    wf_in = nc.dram_tensor("wf", [D, D], F32, kind="ExternalInput")
    bf_in = nc.dram_tensor("bf", [D], F32, kind="ExternalInput")
    af_in = nc.dram_tensor("af", [D, RANK], F32, kind="ExternalInput")
    bm_in = nc.dram_tensor("bm", [RANK, D], F32, kind="ExternalInput")
    lim_in = nc.dram_tensor("lim", [E, RANK], F32, kind="ExternalInput")
    sc_in = nc.dram_tensor("sincos", [2, 128, TL], F32, kind="ExternalInput")
    out_t = nc.dram_tensor("out", [TL, D], F32, kind="ExternalOutput")

    wt_dram = nc.dram_tensor("wt_dram", [D, D], BF16)
    g_dram = nc.dram_tensor("g_dram", [TL, D], BF16)
    ar_in = nc.dram_tensor("ar_in", [2, 1], F32)
    ar_out = nc.dram_tensor("ar_out", [2, 1], F32)

    ident_bf = nc.inline_tensor(np.eye(128, dtype=BF), name="ident_bf")
    ident_f = nc.inline_tensor(np.eye(128, dtype=np.float32), name="ident_f")
    iota8_c = nc.inline_tensor(np.tile(np.arange(8, dtype=np.float32), (128, 1)), name="iota8")
    ones1f_c = nc.inline_tensor(np.ones((1, 128), np.float32), name="ones1f")

    es = contextlib.ExitStack()
    sb = lambda *a: es.enter_context(nc.sbuf_tensor(*a))
    psr = lambda *a: es.enter_context(nc.psum_tensor(*a))

    arenaA = sb("arenaA", [128, DC * TL], BF16)
    arenaB = sb("arenaB", [128, DC * TL], BF16)
    ktok = sb("ktok", [128, NT * KVD], BF16)
    vtok = sb("vtok", [128, NT * KVD], BF16)
    wslab = sb("wslab", [128, 8192], BF16)
    prodb = sb("prodb", [128, 4096], BF16)

    ttmp = sb("ttmp", [128, 1024], BF16)
    ttmp2 = sb("ttmp2", [128, 1024], BF16)
    gstr = sb("gstr", [128, 2 * NT * 128], BF16)
    xstr = sb("xstr", [128, 2 * NT * 128], F32)
    ostr = sb("ostr", [128, NT * 128], F32)
    sumx = sb("sumx", [128, 1], F32)
    ssq = sb("ssq", [128, 1], F32)
    mven = sb("mven", [128, 4], F32)
    mx8 = sb("mx8", [E, 2], F32)
    mx2 = sb("mx2", [2, 1], F32)
    s12r = sb("s12r", [1, 2], F32)
    s12b = sb("s12b", [128, 2], F32)
    w8t = sb("w8t", [128, DC * E], BF16)
    w8row = prodb[0:E, 0:4096]
    b8c = sb("b8c", [E, 1], F32)
    bcol = sb("bcol", [128, DC], F32)
    bcolt = sb("bcolt", [RANK, 128], F32)
    limbf = sb("limbf", [E, RANK], BF16)
    limtbf = sb("limtbf", [RANK, E], BF16)
    bmsl = sb("bmsl", [RANK, 2 * 128], BF16)
    idbf = sb("idbf", [128, 128], BF16)
    idf = sb("idf", [128, 128], F32)
    io8 = sb("io8", [128, 8], F32)
    on1f = sb("on1f", [1, 128], F32)
    gt_a = sb("gt_a", [128, 8], F32)
    gt_b = sb("gt_b", [128, 8], F32)
    gt_c = sb("gt_c", [128, 8], F32)
    gt_d = sb("gt_d", [128, 8], F32)
    gt_e = sb("gt_e", [128, 8], BF16)
    gs_a = sb("gs_a", [128, 1], F32)
    gs_b = sb("gs_b", [128, 1], F32)
    gs_c = sb("gs_c", [128, 1], F32)
    vsp = sb("vsp", [128, 1], F32)
    scsb = prodb[:].bitcast(F32)
    zt = ostr[0:RANK, 0:TL]
    zanch = prodb[0:RANK, 2048:2560]
    wft = prodb[0:E, 3072:4096]
    gmul = prodb[0:RANK, 1024:2048]
    abf = gstr[:, 0:DC * RANK]
    hsl = xstr[0:E, 0:512]
    dsl = xstr[0:E, 512:1024]
    sc8 = ttmp[:, 0:512].bitcast(F32)
    e8 = ttmp[:, 512:1024].bitcast(F32)
    p8 = ttmp2[:, 0:256]
    sig8 = ttmp2[:, 256:320].bitcast(F32)
    rsg8 = ttmp2[:, 320:384].bitcast(F32)
    rp_a = xstr[:, 0:1024]
    rp_b = xstr[:, 1024:2048]

    pj = [psr(f"pj{i}", [128, 512], F32) for i in range(4)]
    pT = psr("pT", [128, 2048], BF16)
    pg = psr("pg", [128, 512], F32)
    ph = psr("ph", [128, 512], F32)

    r = Rec()
    for nm in ("LW", "LS", "MM", "VE", "SE", "CC", "ST"):
        r.h[nm] = es.enter_context(nc.semaphore(nm))

    def dma(eng, qsem, out, in_):
        v = r.inc(qsem, 16)
        h = r.h[qsem]
        r.emit(eng, lambda e, o=out, i=in_, h=h: e.dma_start(out=o, in_=i).then_inc(h, 16))
        return v

    def emit(eng, fn):
        r.emit(eng, fn)

    def emit_inc(eng, fn, sem_name):
        v = r.inc(sem_name)
        h = r.h[sem_name]
        r.emit(eng, lambda e, f=fn, h=h: f(e).then_inc(h, 1))
        return v

    def vgap(n=2):
        for _ in range(n):
            r.emit("v", lambda e: e.memset(vsp[:], 0.0))

    def emit_inc_dma_safe(fn, sem_name, gaps=3):
        r.emit("v", fn)
        vgap(gaps)
        return emit_inc("v", lambda e: e.memset(vsp[:], 0.0), sem_name)

    xnT3 = arenaA[:].rearrange("p (dc t) -> p dc t", dc=DC)

    dma("g", "LW", idbf[:], ident_bf[:, :])
    dma("g", "LW", idf[:], ident_f[:, :])
    dma("g", "LW", io8[:], iota8_c[:, :])
    dma("g", "LW", on1f[:], ones1f_c[:, :])
    dma("g", "LW", limbf[:], lim_in[:, :])
    dma("g", "LW", abf[:].rearrange("p (dc rr) -> p dc rr", dc=DC),
        af_in.ap().rearrange("(dc p) rr -> p dc rr", p=128))
    dma("g", "LW", w8row[:, :], wf_in[0:E, :])
    dma("g", "LW", b8c[:], bf_in.ap()[0:E].unsqueeze(1))
    v_consts = dma("g", "LW", bcolt[:], bf_in.ap().rearrange("(dc p) -> dc p", p=128))

    sq_scr = arenaB[:, 0:8192].bitcast(F32)
    x_ld = [arenaB[:, 8192 + j * 8192: 8192 + (j + 1) * 8192].bitcast(F32) for j in range(2)]
    xn_tok = arenaB[:, 24576:24576 + 4096]

    v_xload = [None] * NT
    v_tile_users = [0] * NT
    v_tile_se = [0] * NT
    v_xload[0] = dma("y", "LS", x_ld[0][:, :], x_in[0:128, :])
    v_xload[1] = dma("y", "LS", x_ld[1][:, :], x_in[128:256, :])

    for i in range(NT):
        xb = x_ld[i % 2]
        r.wait("s", "LS", v_xload[i])
        if i >= 1:
            r.wait("s", "VE")
        emit("s", lambda e, xb=xb: e.activation(
            sq_scr[:, :], xb[:, :], AF.Square, accum_out=ssq[:]))
        v_se_sum = emit_inc("s", lambda e, xb=xb: e.activation(
            sq_scr[:, :], xb[:, :], AF.Copy, accum_out=sumx[:]), "SE")
        r.wait("v", "SE", v_se_sum)
        emit("v", lambda e: e.tensor_single_scalar(mven[:, 0:1], sumx[:], -1.0 / D, ALU.mult))
        vgap()
        emit("v", lambda e: e.tensor_scalar(
            mven[:, 1:2], mven[:, 0:1], mven[:, 0:1], 1e-5, ALU.mult, ALU.subtract))
        vgap()
        v_var = emit_inc("v", lambda e: e.scalar_tensor_tensor(
            mven[:, 1:2], ssq[:], 1.0 / D, mven[:, 1:2], ALU.mult, ALU.subtract), "VE")
        r.wait("s", "VE", v_var)
        v_std = emit_inc("s", lambda e: e.activation(mven[:, 2:3], mven[:, 1:2], AF.Sqrt,
                                                     scale=1.0), "SE")
        v_tile_se[i] = v_std
        r.wait("v", "SE", v_std)
        emit("v", lambda e: e.reciprocal(mven[:, 3:4], mven[:, 2:3]))
        vgap()
        v_xn = emit_inc("v", lambda e, xb=xb: e.tensor_scalar(
            xn_tok[:, :], xb[:, :], mven[:, 0:1], mven[:, 3:4], ALU.add, ALU.mult), "VE")
        v_tile_users[i] = v_xn
        if i + 2 < NT:
            r.wait("y", "VE", v_xn)
            r.wait("y", "SE", v_se_sum)
            v_xload[i + 2] = dma("y", "LS", x_ld[i % 2][:, :],
                                 x_in[(i + 2) * 128:(i + 3) * 128, :])
        for rnd in range(2):
            r.wait("t", "VE")
            for j in range(16):
                dc = rnd * 16 + j
                fn = lambda e, dc=dc, j=j: e.transpose(
                    pT[:, j * 128:(j + 1) * 128], xn_tok[:, dc * 128:(dc + 1) * 128], idbf[:, :])
                (emit if j < 15 else (lambda eng, f: emit_inc(eng, f, "MM")))("t", fn)
            r.wait("v", "MM")
            base = rnd * 16
            emit_inc("v", lambda e, base=base, i=i: e.tensor_copy(
                xnT3[:, base:base + 16, i * 128:(i + 1) * 128],
                pT[:].rearrange("p (j f) -> p j f", j=16)), "VE")

    v_xnT_done = r.val("VE")

    wstage = [wslab[:, 0:4096], wslab[:, 4096:8192]]
    wstrip = [arenaB[:, 28672:32768], arenaB[:, 28672:32768]]
    v_wblk = [0] * DC
    v_wuse = [0] * DC
    v_wstore = [0] * DC
    v_wevac = [0] * DC
    v_wblk[0] = dma("g", "LW", wstage[0][:, :], wf_in[0:128, :])
    v_wblk[1] = dma("g", "LW", wstage[1][:, :], wf_in[128:256, :])
    for ob in range(DC):
        src = wstage[ob % 2]
        strip = wstrip[ob % 2]
        for rnd in range(2):
            r.wait("t", "LW", v_wblk[ob])
            r.wait("t", "VE")
            for j in range(16):
                dc = rnd * 16 + j
                fn = lambda e, src=src, dc=dc, j=j: e.transpose(
                    pT[:, j * 128:(j + 1) * 128], src[:, dc * 128:(dc + 1) * 128], idbf[:, :])
                (emit if j < 15 else (lambda eng, f: emit_inc(eng, f, "MM")))("t", fn)
            r.wait("v", "MM")
            if ob >= 1 and rnd == 0:
                r.wait("v", "LS", v_wstore[ob - 1])
            base = rnd * 16
            v_wevac[ob] = emit_inc_dma_safe(lambda e, strip=strip, base=base: e.tensor_copy(
                strip[:].rearrange("p (dc f) -> p dc f", dc=DC)[:, base:base + 16, :],
                pT[:].rearrange("p (j f) -> p j f", j=16)), "VE")
        v_wuse[ob] = r.val("MM")
        r.wait("y", "VE", v_wevac[ob])
        v_wstore[ob] = dma("y", "LS", wt_dram.ap().rearrange(
            "(dc p) o -> p dc o", p=128)[:, :, ob * 128:(ob + 1) * 128],
            strip[:].rearrange("p (dc f) -> p dc f", dc=DC))
        if ob + 2 < DC:
            r.wait("g", "MM", v_wuse[ob])
            v_wblk[ob + 2] = dma("g", "LW", wstage[ob % 2][:, :],
                                 wf_in[(ob + 2) * 128:(ob + 3) * 128, :])
    v_wstage_all_mm = r.val("MM")
    v_wstage_all_st = r.val("LS")

    r.wait("t", "LW", v_consts)
    r.wait("t", "VE")
    for dc in range(DC):
        fn = lambda e, dc=dc: e.transpose(pT[:, dc * 8:(dc + 1) * 8],
                                          w8row[0:E, dc * 128:(dc + 1) * 128], idbf[0:E, 0:E])
        (emit if dc < DC - 1 else (lambda eng, f: emit_inc(eng, f, "MM")))("t", fn)
    emit_inc("t", lambda e: e.transpose(pT[0:RANK, 1024:1024 + 8], limbf[0:E, :], idbf[0:E, 0:E]), "MM")
    r.wait("v", "MM")
    emit("v", lambda e: e.tensor_copy(w8t[:], pT[:, 0:DC * E]))
    emit_inc("v", lambda e: e.tensor_copy(limtbf[:], pT[0:RANK, 1024:1024 + 8]), "VE")
    r.wait("t", "VE")
    emit_inc("t", lambda e: e.transpose(pg[:, 0:DC], bcolt[:], idf[0:RANK, 0:RANK]), "MM")
    r.wait("v", "MM")
    emit_inc("v", lambda e: e.tensor_copy(bcol[:], pg[:, 0:DC]), "VE")

    r.wait("t", "VE")
    for tq in range(2):
        if tq:
            r.wait("t", "VE")
        for dc in range(DC):
            fn = lambda e, tq=tq, dc=dc: e.matmul(
                pg[0:RANK, :], abf[:, dc * RANK:(dc + 1) * RANK],
                arenaA[:, dc * TL + tq * 512: dc * TL + tq * 512 + 512],
                start=(dc == 0), stop=(dc == DC - 1))
            (emit if dc < DC - 1 else (lambda eng, f: emit_inc(eng, f, "MM")))("t", fn)
        r.wait("v", "MM")
        emit_inc("v", lambda e, tq=tq: e.tensor_copy(zt[:, tq * 512:(tq + 1) * 512], pg[0:RANK, :]), "VE")
    for tq in range(2):
        if tq:
            r.wait("t", "VE")
        for dc in range(DC):
            fn = lambda e, tq=tq, dc=dc: e.matmul(
                ph[0:E, :], w8t[:, dc * E:(dc + 1) * E],
                arenaA[:, dc * TL + tq * 512: dc * TL + tq * 512 + 512],
                start=(dc == 0), stop=(dc == DC - 1))
            (emit if dc < DC - 1 else (lambda eng, f: emit_inc(eng, f, "MM")))("t", fn)
        r.wait("v", "MM")
        emit_inc("v", lambda e, tq=tq: e.tensor_scalar_add(
            hsl[:, tq * 256:(tq + 1) * 256],
            ph[0:E, :].rearrange("p (a two) -> p a two", two=2)[:, :, 1], b8c[:]), "VE")
    v_zanch = emit_inc("v", lambda e: e.tensor_copy(
        zanch[:], zt[:].rearrange("p (a two) -> p a two", two=2)[:, :, 1]), "VE")
    r.wait("t", "VE", v_zanch)
    emit_inc("t", lambda e: e.matmul(pg[0:E, :], limtbf[:], zanch[:], start=True, stop=True), "MM")
    r.wait("v", "MM")
    emit("v", lambda e: e.tensor_copy(dsl[:], pg[0:E, :]))
    emit("v", lambda e: e.reduce_max(mx8[:, 0:1], hsl[:], axis=AX.X, apply_absolute_value=True))
    v_mx = emit_inc("v", lambda e: e.reduce_max(mx8[:, 1:2], dsl[:], axis=AX.X,
                                                apply_absolute_value=True), "VE")
    r.wait("t", "VE", v_mx)
    emit_inc("t", lambda e: e.transpose(ph[0:2, 0:E], mx8[0:E, :], idf[0:E, 0:E]), "MM")
    r.wait("v", "MM")
    v_mx2 = emit_inc_dma_safe(lambda e: e.reduce_max(mx2[:], ph[0:2, 0:E], axis=AX.X), "VE")
    r.wait("g", "VE", v_mx2)
    v_arin = dma("g", "LW", ar_in[:, :], mx2[:])
    r.wait("g", "LW", v_arin)
    v_cc = r.inc("CC", 1)
    emit("g", lambda e: e.collective_compute(
        "AllReduce", ALU.max, replica_groups=[list(range(NCORE))],
        ins=[ar_in.ap().opt()], outs=[ar_out.ap().opt()]).then_inc(r.h["CC"], 1))
    r.wait("g", "CC", v_cc)
    v_armax = dma("g", "LW", mx2[:], ar_out[:, :])
    r.wait("v", "LW", v_armax)
    emit("v", lambda e: e.tensor_single_scalar(mx2[:], mx2[:], 1e-6, ALU.max))
    vgap()
    emit("v", lambda e: e.reciprocal(mx2[:], mx2[:]))
    vgap()
    v_sc = emit_inc("v", lambda e: e.tensor_single_scalar(mx2[:], mx2[:], 0.5, ALU.mult), "VE")
    r.wait("t", "VE", v_sc)
    emit_inc("t", lambda e: e.transpose(pg[0:1, 0:2], mx2[0:2, :], idf[0:2, 0:2]), "MM")
    r.wait("v", "MM")
    v_s12 = emit_inc("v", lambda e: e.tensor_copy(s12r[:], pg[0:1, 0:2]), "VE")
    r.wait("t", "VE", v_s12)
    emit_inc("t", lambda e: e.matmul(ph[:, 0:2], on1f[:], s12r[:], start=True, stop=True), "MM")
    r.wait("v", "MM")
    emit_inc("v", lambda e: e.tensor_copy(s12b[:], ph[:, 0:2]), "VE")
    vgap()

    for at in range(4):
        r.wait("t", "VE")
        emit("t", lambda e, at=at: e.transpose(pg[:, 0:E], hsl[0:E, at * 128:(at + 1) * 128],
                                               idf[0:E, 0:E]))
        emit_inc("t", lambda e, at=at: e.transpose(ph[:, 0:E], dsl[0:E, at * 128:(at + 1) * 128],
                                                   idf[0:E, 0:E]), "MM")
        r.wait("v", "MM")
        emit("v", lambda e: e.tensor_scalar_mul(gt_a[:], pg[:, 0:E], s12b[:, 0:1]))
        emit("v", lambda e: e.tensor_scalar_mul(gt_b[:], ph[:, 0:E], s12b[:, 1:2]))
        vgap()
        emit("v", lambda e: e.tensor_tensor(gt_a[:], gt_a[:], gt_b[:], ALU.add))
        vgap()
        v1 = emit_inc("v", lambda e: e.reduce_max(gs_a[:], gt_a[:], axis=AX.X, negate=True), "VE")
        r.wait("s", "VE", v1)
        v2 = emit_inc("s", lambda e: e.activation(gt_b[:], gt_a[:], AF.Exp, bias=gs_a[:],
                                                  scale=1.0, accum_out=gs_b[:]), "SE")
        r.wait("v", "SE", v2)
        emit("v", lambda e: e.reciprocal(gs_b[:], gs_b[:]))
        vgap()
        emit("v", lambda e: e.tensor_scalar_mul(gt_b[:], gt_b[:], gs_b[:]))
        vgap()
        emit("v", lambda e: e.reduce_max(gs_a[:], gt_b[:], axis=AX.X))
        vgap()
        emit("v", lambda e: e.tensor_scalar_sub(gt_c[:], gt_b[:], gs_a[:]))
        vgap()
        emit("v", lambda e: e.tensor_single_scalar(gt_c[:], gt_c[:], 0.0, ALU.is_equal))
        vgap()
        emit("v", lambda e: e.tensor_single_scalar(gt_c[:], gt_c[:], 1.0, ALU.subtract))
        vgap()
        emit("v", lambda e: e.scalar_tensor_tensor(gt_d[:], gt_c[:], -1e30, io8[:],
                                                   ALU.mult, ALU.add))
        vgap()
        emit("v", lambda e: e.tensor_reduce(gs_a[:], gt_d[:], axis=AX.X, op=ALU.min))
        vgap()
        emit("v", lambda e: e.tensor_scalar_sub(gt_c[:], io8[:], gs_a[:]))
        vgap()
        emit("v", lambda e: e.tensor_single_scalar(gt_c[:], gt_c[:], 0.0, ALU.is_equal))
        vgap()
        emit("v", lambda e: e.scalar_tensor_tensor(gt_d[:], gt_c[:], -1e30, gt_b[:],
                                                   ALU.mult, ALU.add))
        vgap()
        emit("v", lambda e: e.reduce_max(gs_a[:], gt_d[:], axis=AX.X))
        vgap()
        v3 = emit_inc("v", lambda e: e.tensor_single_scalar(gs_a[:], gs_a[:], -2.0, ALU.mult), "VE")
        r.wait("s", "VE", v3)
        v4 = emit_inc("s", lambda e: e.activation(gt_c[:], gt_b[:], AF.Sigmoid, bias=gs_a[:],
                                                  scale=2.0), "SE")
        r.wait("v", "SE", v4)
        emit("v", lambda e: e.tensor_tensor(gt_d[:], gt_b[:], gt_c[:], ALU.mult))
        vgap()
        emit("v", lambda e: e.reduce_sum(gs_c[:], gt_d[:], axis=AX.X))
        vgap()
        emit("v", lambda e: e.tensor_single_scalar(gs_c[:], gs_c[:], 1e-9, ALU.add))
        vgap()
        emit("v", lambda e: e.reciprocal(gs_c[:], gs_c[:]))
        vgap()
        v7 = emit_inc("v", lambda e: e.tensor_scalar_mul(gt_e[:], gt_d[:], gs_c[:]), "VE")
        r.wait("t", "VE", v7)
        emit_inc("t", lambda e: e.transpose(pT[0:E, 0:128], gt_e[:], idbf[:, :]), "MM")
        r.wait("v", "MM")
        emit_inc("v", lambda e, at=at: e.tensor_copy(
            wft[:, at * 256:(at + 1) * 256].rearrange("p (a b) -> p a b", b=2),
            pT[0:E, 0:128].unsqueeze(2).broadcast_to([E, 128, 2])), "VE")

    for tq in range(2):
        r.wait("t", "VE")
        emit_inc("t", lambda e, tq=tq: e.matmul(pg[0:RANK, :], limbf[0:E, :],
                                                wft[:, tq * 512:(tq + 1) * 512],
                                                start=True, stop=True), "MM")
        r.wait("v", "MM")
        emit_inc("v", lambda e, tq=tq: e.scalar_tensor_tensor(
            gmul[:, tq * 512:(tq + 1) * 512], zt[:, tq * 512:(tq + 1) * 512], 0.5,
            pg[0:RANK, :], ALU.mult, ALU.mult), "VE")
    v_gmul = r.val("VE")

    hslab = [wslab[:, 0:4096], wslab[:, 4096:8192]]
    gt_tmp = [ttmp, ttmp2]
    v_hblk = [0] * DC
    v_hcomp = [0] * DC
    v_hsilu = [0] * DC
    v_gevac = [0] * DC
    v_gstore = [0] * DC
    r.wait("y", "MM", v_wstage_all_mm)
    v_bm = [0] * DC
    v_bm[0] = dma("g", "LW", bmsl[:, 0:128], bm_in[:, 0:128])
    v_bm[1] = dma("g", "LW", bmsl[:, 128:256], bm_in[:, 128:256])
    v_hblk[0] = dma("y", "LS", hslab[0][:].rearrange("p (dc o) -> p dc o", dc=DC),
                    wt_dram.ap().rearrange("(dc p) o -> p dc o", p=128)[:, :, 0:128])
    v_hblk[1] = dma("y", "LS", hslab[1][:].rearrange("p (dc o) -> p dc o", dc=DC),
                    wt_dram.ap().rearrange("(dc p) o -> p dc o", p=128)[:, :, 128:256])
    for ob in range(DC):
        r.wait("t", "LS", v_hblk[ob])
        r.wait("t", "LW", v_bm[ob])
        if ob == 0:
            r.wait("t", "VE", v_gmul)
        banks = [pj[(ob % 2) * 2], pj[(ob % 2) * 2 + 1]]
        for tq in range(2):
            bank = banks[tq]
            for dc in range(DC):
                fn = lambda e, ob=ob, tq=tq, dc=dc, bank=bank: e.matmul(
                    bank[:], hslab[ob % 2][:, dc * 128:(dc + 1) * 128],
                    arenaA[:, dc * TL + tq * 512: dc * TL + tq * 512 + 512],
                    start=(dc == 0), stop=False)
                emit("t", fn)
            fn = lambda e, ob=ob, tq=tq, bank=bank: e.matmul(
                bank[:], bmsl[:, (ob % 2) * 128:(ob % 2) * 128 + 128],
                gmul[:, tq * 512:(tq + 1) * 512], start=False, stop=True)
            emit_inc("t", fn, "MM")
            r.wait("s", "MM")
            v_hsilu[ob] = emit_inc("s", lambda e, ob=ob, tq=tq, bank=bank: e.activation(
                gt_tmp[ob % 2][:, tq * 512:(tq + 1) * 512], bank[:], AF.Silu,
                bias=bcol[:, ob:ob + 1]), "SE")
        v_hcomp[ob] = r.val("MM")
        r.wait("t", "SE", v_hsilu[ob])
        r.wait("t", "VE")
        for j in range(NT):
            fn = lambda e, ob=ob, j=j: e.transpose(
                pT[:, j * 128:(j + 1) * 128], gt_tmp[ob % 2][:, j * 128:(j + 1) * 128], idbf[:, :])
            (emit if j < NT - 1 else (lambda eng, f: emit_inc(eng, f, "MM")))("t", fn)
        r.wait("v", "MM")
        gsl = gstr[:, (ob % 2) * 1024:(ob % 2) * 1024 + 1024]
        if ob >= 2:
            r.wait("v", "LS", v_gstore[ob - 2])
        v_gevac[ob] = emit_inc_dma_safe(lambda e, gsl=gsl: e.tensor_copy(gsl[:, :], pT[:, 0:NT * 128]), "VE")
        r.wait("y", "VE", v_gevac[ob])
        v_gstore[ob] = dma("y", "LS",
                           g_dram.ap().rearrange("(i p) o -> p i o", p=128)[:, :, ob * 128:(ob + 1) * 128],
                           gsl[:].rearrange("p (i f) -> p i f", i=NT))
        if ob + 2 < DC:
            r.wait("y", "MM", v_hcomp[ob])
            v_hblk[ob + 2] = dma("y", "LS", hslab[ob % 2][:].rearrange("p (dc o) -> p dc o", dc=DC),
                                 wt_dram.ap().rearrange("(dc p) o -> p dc o", p=128)[:, :, (ob + 2) * 128:(ob + 3) * 128])
            r.wait("g", "MM", v_hcomp[ob])
            v_bm[ob + 2] = dma("g", "LW", bmsl[:, (ob % 2) * 128:(ob % 2) * 128 + 128],
                               bm_in[:, (ob + 2) * 128:(ob + 3) * 128])
    v_h_done_mm = r.val("MM")
    r.wait("y", "MM", v_h_done_mm)
    v_sincos = dma("y", "LS", scsb[:].rearrange("p (s f) -> p s f", s=2),
                   sc_in.ap().rearrange("s p f -> p s f"))

    qslab = [wslab[:, 0:4096], wslab[:, 4096:8192]]

    def emit_projection(w_dram, nblocks, dst_tok, dstw, rope, first_gate_mm):
        v_ld = [0] * nblocks
        v_cmp = [0] * nblocks
        r.wait("g", "MM", first_gate_mm)
        v_ld[0] = dma("g", "LW", qslab[0][:].rearrange("p (dc o) -> p dc o", dc=DC),
                      w_dram.ap().rearrange("(dc p) o -> p dc o", p=128)[:, :, 0:128])
        if nblocks > 1:
            v_ld[1] = dma("g", "LW", qslab[1][:].rearrange("p (dc o) -> p dc o", dc=DC),
                          w_dram.ap().rearrange("(dc p) o -> p dc o", p=128)[:, :, 128:256])
        for ob in range(nblocks):
            r.wait("t", "LW", v_ld[ob])
            banks = [pj[(ob % 2) * 2], pj[(ob % 2) * 2 + 1]]
            for tq in range(2):
                bank = banks[tq]
                for dc in range(DC):
                    fn = lambda e, ob=ob, tq=tq, dc=dc, bank=bank: e.matmul(
                        bank[:], qslab[ob % 2][:, dc * 128:(dc + 1) * 128],
                        arenaA[:, dc * TL + tq * 512: dc * TL + tq * 512 + 512],
                        start=(dc == 0), stop=(dc == DC - 1))
                    (emit if dc < DC - 1 else (lambda eng, f: emit_inc(eng, f, "MM")))("t", fn)
            v_cmp[ob] = r.val("MM")
            r.wait("v", "MM")
            emit("v", lambda e, banks=banks: e.tensor_copy(ttmp[:, 0:512], banks[0][:]))
            v_ev = emit_inc("v", lambda e, banks=banks: e.tensor_copy(ttmp[:, 512:1024], banks[1][:]), "VE")
            r.wait("t", "VE", v_ev)
            for j in range(NT):
                fn = lambda e, j=j: e.transpose(pT[:, j * 128:(j + 1) * 128],
                                                ttmp[:, j * 128:(j + 1) * 128], idbf[:, :])
                (emit if j < NT - 1 else (lambda eng, f: emit_inc(eng, f, "MM")))("t", fn)
            r.wait("v", "MM")
            dst3 = dst_tok[:].rearrange("p (i w) -> p i w", i=NT)
            if not rope:
                emit_inc("v", lambda e, dst3=dst3, ob=ob: e.tensor_copy(
                    dst3[:, :, ob * 128:(ob + 1) * 128],
                    pT[:, 0:NT * 128].rearrange("p (i f) -> p i f", i=NT)), "VE")
            else:
                pt3 = pT[:, 0:NT * 128].rearrange("p (i f) -> p i f", i=NT)
                rpa3 = rp_a[:].rearrange("p (i f) -> p i f", i=NT)
                sin3 = scsb[:, TL:2 * TL].rearrange("p (i f) -> p i f", i=NT)
                emit("v", lambda e, pt3=pt3, rpa3=rpa3, sin3=sin3: e.tensor_tensor(
                    rpa3[:, :, 0:64], pt3[:, :, 64:128], sin3[:, :, 0:64], ALU.mult))
                emit("v", lambda e, pt3=pt3, rpa3=rpa3, sin3=sin3: e.tensor_tensor(
                    rpa3[:, :, 64:128], pt3[:, :, 0:64], sin3[:, :, 64:128], ALU.mult))
                emit("v", lambda e: e.tensor_tensor(rp_b[:], pT[:, 0:1024], scsb[:, 0:TL], ALU.mult))
                emit_inc("v", lambda e, dst3=dst3, ob=ob: e.tensor_tensor(
                    dst3[:, :, ob * 128:(ob + 1) * 128],
                    rp_a[:].rearrange("p (i f) -> p i f", i=NT),
                    rp_b[:].rearrange("p (i f) -> p i f", i=NT), ALU.add), "VE")
            if ob + 2 < nblocks:
                r.wait("g", "MM", v_cmp[ob])
                v_ld[ob + 2] = dma("g", "LW", qslab[ob % 2][:].rearrange("p (dc o) -> p dc o", dc=DC),
                                   w_dram.ap().rearrange("(dc p) o -> p dc o", p=128)[:, :, (ob + 2) * 128:(ob + 3) * 128])
        return r.val("MM")

    r.wait("v", "LS", v_sincos)
    v_after_k = emit_projection(wk_in, HKV, ktok, KVD, True, v_h_done_mm)
    v_after_v = emit_projection(wv_in, HKV, vtok, KVD, False, v_after_k)
    v_after_q = emit_projection(wq_in, DC, arenaB, D, True, v_after_v)
    v_qkv_done = r.val("VE")

    v_exp_prev = 0
    for i in range(NT):
        qv = arenaB[:, i * D:(i + 1) * D]
        kv = ktok[:, i * KVD:(i + 1) * KVD]
        vv = vtok[:, i * KVD:(i + 1) * KVD]
        if i == 0:
            r.wait("v", "VE", v_qkv_done)
        if v_exp_prev:
            r.wait("v", "SE", v_exp_prev)
        for hg in range(8):
            pb = prodb[:, 0:4096]
            emit("v", lambda e, qv=qv, kv=kv, pb=pb, hg=hg: e.tensor_tensor(
                pb[:].rearrange("p (h g d) -> p h g d", h=4, g=E),
                qv[:, hg * 512:(hg + 1) * 512].rearrange("p (h d) -> p h d", h=4)
                  .unsqueeze(2).broadcast_to([128, 4, E, DH]),
                kv[:].rearrange("p (g d) -> p g d", g=E)
                  .unsqueeze(1).broadcast_to([128, 4, E, DH]), ALU.mult))
            fn = lambda e, pb=pb, hg=hg: e.tensor_reduce(
                sc8[:, hg * 32:(hg + 1) * 32].rearrange("p (h g) -> p h g", h=4).unsqueeze(3),
                pb[:].rearrange("p (h g d) -> p h g d", h=4, g=E),
                axis=AX.X, op=ALU.add)
            if hg < 7:
                emit("v", fn)
            else:
                emit_inc("v", fn, "VE")
        r.wait("s", "VE")
        v_exp_prev = emit_inc("s", lambda e: e.activation(e8[:], sc8[:], AF.Exp, scale=RSQ), "SE")
        r.wait("v", "SE", v_exp_prev)
        emit("v", lambda e: e.tensor_reduce(
            sig8[:].unsqueeze(2), e8[:].rearrange("p (h g) -> p h g", h=HQ),
            axis=AX.X, op=ALU.add))
        vgap()
        emit("v", lambda e: e.reciprocal(rsg8[:], sig8[:]))
        vgap()
        emit("v", lambda e: e.tensor_tensor(
            p8[:].rearrange("p (h g) -> p h g", h=HQ),
            e8[:].rearrange("p (h g) -> p h g", h=HQ),
            rsg8[:].unsqueeze(2).broadcast_to([128, HQ, E]), ALU.mult))
        vgap()
        for hg in range(8):
            pb = prodb[:, 0:4096]
            emit("v", lambda e, vv=vv, pb=pb, hg=hg: e.tensor_tensor(
                pb[:].rearrange("p (h d g) -> p h d g", h=4, d=DH),
                p8[:, hg * 32:(hg + 1) * 32].rearrange("p (h g) -> p h g", h=4)
                  .unsqueeze(2).broadcast_to([128, 4, DH, E]),
                vv[:].rearrange("p (g d) -> p g d", g=E).transpose([0, 2, 1])
                  .unsqueeze(1).broadcast_to([128, 4, DH, E]), ALU.mult))
            def fn(e, qv=qv, pb=pb, hg=hg):
                with nc.allow_low_precision(reason="attnv 8-term reduce, fp32 internal"):
                    return e.tensor_reduce(
                        qv[:, hg * 512:(hg + 1) * 512].rearrange("p (h d) -> p h d", h=4).unsqueeze(3),
                        pb[:].rearrange("p (h d g) -> p h d g", h=4, d=DH),
                        axis=AX.X, op=ALU.add)
            if hg < 7:
                emit("v", fn)
            else:
                emit_inc("v", fn, "VE")
        v_ao = r.val("VE")
        for rnd in range(2):
            r.wait("t", "VE")
            for j in range(16):
                dc = rnd * 16 + j
                fn = lambda e, qv=qv, dc=dc, j=j: e.transpose(
                    pT[:, j * 128:(j + 1) * 128], qv[:, dc * 128:(dc + 1) * 128], idbf[:, :])
                (emit if j < 15 else (lambda eng, f: emit_inc(eng, f, "MM")))("t", fn)
            r.wait("v", "MM")
            base = rnd * 16
            emit_inc("v", lambda e, base=base, i=i: e.tensor_copy(
                xnT3[:, base:base + 16, i * 128:(i + 1) * 128],
                pT[:].rearrange("p (j f) -> p j f", j=16)), "VE")
    v_att_done = r.val("VE")
    v_att_done_mm = r.val("MM")

    v_ld = [0] * DC
    v_cmp = [0] * DC
    v_gld = [0] * DC
    v_xld2 = [0] * DC
    v_comb = [0] * DC
    v_ost = [0] * DC
    r.wait("g", "MM", v_after_q)
    v_ld[0] = dma("g", "LW", qslab[0][:].rearrange("p (dc o) -> p dc o", dc=DC),
                  wo_in.ap().rearrange("(dc p) o -> p dc o", p=128)[:, :, 0:128])
    v_ld[1] = dma("g", "LW", qslab[1][:].rearrange("p (dc o) -> p dc o", dc=DC),
                  wo_in.ap().rearrange("(dc p) o -> p dc o", p=128)[:, :, 128:256])
    r.wait("y", "VE", v_att_done)
    v_gld[0] = dma("y", "LS", gstr[:, 0:1024].rearrange("p (i f) -> p i f", i=NT),
                   g_dram.ap().rearrange("(i p) o -> p i o", p=128)[:, :, 0:128])
    v_xld2[0] = dma("y", "LS", xstr[:, 0:1024].rearrange("p (i f) -> p i f", i=NT),
                    x_in.ap().rearrange("(i p) o -> p i o", p=128)[:, :, 0:128])
    v_gld[1] = dma("y", "LS", gstr[:, 1024:2048].rearrange("p (i f) -> p i f", i=NT),
                   g_dram.ap().rearrange("(i p) o -> p i o", p=128)[:, :, 128:256])
    v_xld2[1] = dma("y", "LS", xstr[:, 1024:2048].rearrange("p (i f) -> p i f", i=NT),
                    x_in.ap().rearrange("(i p) o -> p i o", p=128)[:, :, 128:256])
    for ob in range(DC):
        r.wait("t", "LW", v_ld[ob])
        if ob == 0:
            r.wait("t", "VE", v_att_done)
        banks = [pj[(ob % 2) * 2], pj[(ob % 2) * 2 + 1]]
        for tq in range(2):
            bank = banks[tq]
            for dc in range(DC):
                fn = lambda e, ob=ob, tq=tq, dc=dc, bank=bank: e.matmul(
                    bank[:], qslab[ob % 2][:, dc * 128:(dc + 1) * 128],
                    arenaA[:, dc * TL + tq * 512: dc * TL + tq * 512 + 512],
                    start=(dc == 0), stop=(dc == DC - 1))
                (emit if dc < DC - 1 else (lambda eng, f: emit_inc(eng, f, "MM")))("t", fn)
        v_cmp[ob] = r.val("MM")
        r.wait("v", "MM")
        emit("v", lambda e, banks=banks: e.tensor_copy(ttmp[:, 0:512], banks[0][:]))
        v_ev = emit_inc("v", lambda e, banks=banks: e.tensor_copy(ttmp[:, 512:1024], banks[1][:]), "VE")
        r.wait("t", "VE", v_ev)
        for j in range(NT):
            fn = lambda e, j=j: e.transpose(pT[:, j * 128:(j + 1) * 128],
                                            ttmp[:, j * 128:(j + 1) * 128], idbf[:, :])
            (emit if j < NT - 1 else (lambda eng, f: emit_inc(eng, f, "MM")))("t", fn)
        r.wait("v", "MM")
        r.wait("v", "LS", max(v_gld[ob], v_xld2[ob]))
        half = ob % 2
        if ob >= 1:
            r.wait("v", "ST", v_ost[ob - 1])
        emit("v", lambda e, half=half: e.tensor_tensor(
            ostr[:], pT[:, 0:NT * 128], gstr[:, half * 1024:half * 1024 + 1024], ALU.mult))
        v_comb[ob] = emit_inc_dma_safe(lambda e, half=half: e.tensor_tensor(
            ostr[:], ostr[:],
            xstr[:, half * 1024:half * 1024 + 1024], ALU.add), "VE")
        r.wait("y", "VE", v_comb[ob])
        v_ost[ob] = r.inc("ST", 16)
        emit("y", lambda e, ob=ob, half=half: e.dma_start(
            out=out_t.ap().rearrange("(i p) o -> p i o", p=128)[:, :, ob * 128:(ob + 1) * 128],
            in_=ostr[:].rearrange("p (i f) -> p i f", i=NT))
            .then_inc(r.h["ST"], 16))
        if ob + 2 < DC:
            r.wait("g", "MM", v_cmp[ob])
            v_ld[ob + 2] = dma("g", "LW", qslab[ob % 2][:].rearrange("p (dc o) -> p dc o", dc=DC),
                               wo_in.ap().rearrange("(dc p) o -> p dc o", p=128)[:, :, (ob + 2) * 128:(ob + 3) * 128])
            r.wait("y", "VE", v_comb[ob])
            v_gld[ob + 2] = dma("y", "LS", gstr[:, half * 1024:half * 1024 + 1024].rearrange("p (i f) -> p i f", i=NT),
                                g_dram.ap().rearrange("(i p) o -> p i o", p=128)[:, :, (ob + 2) * 128:(ob + 3) * 128])
            v_xld2[ob + 2] = dma("y", "LS", xstr[:, half * 1024:half * 1024 + 1024].rearrange("p (i f) -> p i f", i=NT),
                                 x_in.ap().rearrange("(i p) o -> p i o", p=128)[:, :, (ob + 2) * 128:(ob + 3) * 128])

    r.wait("y", "ST")

    with es, nc.Block() as block:
        @block.tensor
        def _(t):
            for fn in r.ops["t"]:
                fn(t)

        @block.vector
        def _(v):
            for fn in r.ops["v"]:
                fn(v)

        @block.scalar
        def _(s):
            for fn in r.ops["s"]:
                fn(s)

        @block.gpsimd
        def _(g):
            for fn in r.ops["g"]:
                fn(g)

        @block.sync
        def _(y):
            for fn in r.ops["y"]:
                fn(y)

    return nc


def _rope_tables():
    import jax
    import jax.numpy as jnp
    cpu = jax.devices("cpu")[0]
    with jax.default_device(cpu):
        i = jnp.arange(64, dtype=jnp.float32)
        emb = 10000.0 ** (2.0 * jnp.floor(i / 2.0) / 64)
        pos = jnp.arange(S, dtype=jnp.float32)
        freqs = pos[:, None] * emb[None, :]
        sin = jnp.concatenate([jnp.sin(freqs), jnp.sin(freqs)], axis=-1)
        cos = jnp.concatenate([jnp.cos(freqs), jnp.cos(freqs)], axis=-1)
        sin = np.asarray(sin, np.float32)
        cos = np.asarray(cos, np.float32)
    sinmod = sin.copy()
    sinmod[:, :64] = -sinmod[:, :64]
    return cos, sinmod


def kernel(L0_attn_hidden_states, L0_attn_Wq, L0_attn_Wk, L0_attn_Wv, L0_attn_Wo,
           L0_ffn_W, L0_ffn_b, L0_ffn_A, L0_ffn_Bmat, L0_ffn_LiMEs):
    x = np.ascontiguousarray(np.asarray(L0_attn_hidden_states, np.float32)[0])
    cos, sinmod = _rope_tables()
    nc = build_nc()
    in_maps = []
    for c in range(NCORE):
        sl = slice(c * TL, (c + 1) * TL)
        ct = cos[sl].reshape(NT, 128, 128).transpose(1, 0, 2).reshape(128, TL)
        st_ = sinmod[sl].reshape(NT, 128, 128).transpose(1, 0, 2).reshape(128, TL)
        in_maps.append({
            "x": np.ascontiguousarray(x[sl]),
            "wq": np.asarray(L0_attn_Wq, np.float32),
            "wk": np.asarray(L0_attn_Wk, np.float32),
            "wv": np.asarray(L0_attn_Wv, np.float32),
            "wo": np.asarray(L0_attn_Wo, np.float32),
            "wf": np.asarray(L0_ffn_W, np.float32),
            "bf": np.asarray(L0_ffn_b, np.float32),
            "af": np.asarray(L0_ffn_A, np.float32),
            "bm": np.asarray(L0_ffn_Bmat, np.float32),
            "lim": np.asarray(L0_ffn_LiMEs, np.float32),
            "sincos": np.ascontiguousarray(np.stack([ct, st_])),
        })
    res = run_bass_kernel_spmd(nc, in_maps, core_ids=list(range(NCORE)))
    out = np.concatenate([res.results[c]["out"] for c in range(NCORE)], axis=0)
    return out[None]
